# version 02
# baseline (speedup 1.0000x reference)
import numpy as np
import ml_dtypes
from contextlib import ExitStack

import concourse.bass as bass
import concourse.mybir as mybir

F32 = mybir.dt.float32
BF16 = mybir.dt.bfloat16
AF = mybir.ActivationFunctionType
ALU = mybir.AluOpType
AX = mybir.AxisListType
bfd = ml_dtypes.bfloat16

RSQRT512 = 1.0 / np.sqrt(512.0)

class Sched:
    COMPUTE = ("pe", "dve", "act", "pool", "sp")
    DMA_QUEUES = ("qs", "qa", "qp")
    DMA_ISSUER = {"qs": "sp", "qa": "act", "qp": "pool"}
    NPOOL = 6

    def __init__(self, nc, es):
        self.nc = nc
        self.es = es
        self.sem = {}
        self.cnt = {e: 0 for e in self.COMPUTE}
        for e in self.COMPUTE:
            self.sem[e] = es.enter_context(nc.semaphore(f"sem_{e}"))
        self.dma_sems = {}
        self.dma_uses = {}
        for q in self.DMA_QUEUES:
            self.dma_sems[q] = [es.enter_context(nc.semaphore(f"sem_{q}{j}"))
                                for j in range(self.NPOOL)]
            self.dma_uses[q] = [0] * self.NPOOL
        self.dma_idx = {q: 0 for q in self.DMA_QUEUES}
        self.regions = {}
        self.ops = []
        self.waited = {e: {} for e in self.COMPUTE}

    def _ev_sem(self, ev):
        if ev[0] == "e":
            return self.sem[ev[1]], ev[2], f"e_{ev[1]}"
        _, q, j, uses = ev
        return self.dma_sems[q][j], 16 * uses, f"d_{q}{j}"

    def _add_wait(self, waits, consumer, ev):
        if ev is None:
            return
        if ev[0] == "e" and ev[1] == consumer:
            return
        sem, val, name = self._ev_sem(ev)
        if self.waited[consumer].get(name, -1) >= val:
            return
        cur = waits.get(name)
        if cur is None or cur[1] < val:
            waits[name] = (sem, val)

    def _dep_waits(self, consumer, reads, writes):
        waits = {}
        for r in reads:
            reg = self.regions.get(r)
            if reg:
                ev = reg["w"]
                if (ev is not None and ev[0] == "e" and ev[1] == consumer
                        and reg.get("hz")):
                    raise AssertionError(
                        f"hazard: same-engine read of reduce/recip output {r}")
                self._add_wait(waits, consumer, ev)
        for w in writes:
            reg = self.regions.get(w)
            if reg:
                self._add_wait(waits, consumer, reg["w"])
                for ev in reg["r"]:
                    self._add_wait(waits, consumer, ev)
        return waits

    def _commit(self, ev, reads, writes):
        for w in writes:
            self.regions[w] = {"w": ev, "r": []}
        for r in reads:
            reg = self.regions.setdefault(r, {"w": None, "r": []})
            reg["r"].append(ev)

    def op(self, engine, fn, reads=(), writes=(), hazard=False):
        waits = self._dep_waits(engine, reads, writes)
        self.cnt[engine] += 1
        ev = ("e", engine, self.cnt[engine])
        for name, (sem, val) in waits.items():
            self.waited[engine][name] = max(self.waited[engine].get(name, -1), val)
        self.ops.append((engine, list(waits.values()), fn, (self.sem[engine], 1)))
        self._commit(ev, reads, writes)
        if hazard:
            for w in writes:
                self.regions[w]["hz"] = True
        return ev

    def dma(self, queue, out_ap, in_ap, reads=(), writes=(), **kw):
        issuer = self.DMA_ISSUER[queue]
        j = self.dma_idx[queue]
        self.dma_idx[queue] = (j + 1) % self.NPOOL
        sem = self.dma_sems[queue][j]
        prev_uses = self.dma_uses[queue][j]
        waits = self._dep_waits(issuer, reads, writes)
        if prev_uses > 0:
            name = f"d_{queue}{j}"
            if self.waited[issuer].get(name, -1) < 16 * prev_uses:
                waits[name] = (sem, 16 * prev_uses)
        for name, (s, val) in waits.items():
            self.waited[issuer][name] = max(self.waited[issuer].get(name, -1), val)
        self.dma_uses[queue][j] += 1
        ev = ("d", queue, j, self.dma_uses[queue][j])

        def fn(eng, out_ap=out_ap, in_ap=in_ap, kw=kw):
            return eng.dma_start(out=out_ap, in_=in_ap, **kw)

        self.ops.append((issuer, list(waits.values()), fn, (sem, 16)))
        self._commit(ev, reads, writes)
        return ev

    def barrier_all(self):
        for e in self.COMPUTE:
            waits = {}
            for f in self.COMPUTE:
                if f != e and self.cnt[f] > 0:
                    self._add_wait(waits, e, ("e", f, self.cnt[f]))
            for q in self.DMA_QUEUES:
                for j in range(self.NPOOL):
                    if self.dma_uses[q][j] > 0:
                        self._add_wait(waits, e, ("d", q, j, self.dma_uses[q][j]))
            for name, (sem, val) in waits.items():
                self.waited[e][name] = max(self.waited[e].get(name, -1), val)
            self.ops.append((e, list(waits.values()), None, None))

    def dma_transpose(self, queue, out_ap, in_ap, reads=(), writes=()):
        assert queue in ("qs", "qa")
        return self.dma(queue, out_ap, in_ap, reads=reads, writes=writes,
                        transpose=True)

    def finalize(self):
        nc = self.nc
        eng_map = {"pe": nc.tensor, "dve": nc.vector, "act": nc.scalar,
                   "pool": nc.gpsimd, "sp": nc.sync}
        by_engine = {e: [] for e in self.COMPUTE}
        for rec in self.ops:
            by_engine[rec[0]].append(rec)
        with nc.Block() as block:
            def section(engine):
                eng = eng_map[engine]
                for _, waits, fn, inc in by_engine[engine]:
                    for sem, val in waits:
                        eng.wait_ge(sem, val)
                    if fn is None:
                        continue
                    inst = fn(eng)
                    if inc is not None:
                        inst.then_inc(inc[0], inc[1])
                if engine == "pool":
                    for q in self.DMA_QUEUES:
                        for jj, uses in enumerate(self.dma_uses[q]):
                            if uses > 0:
                                eng.wait_ge(self.dma_sems[q][jj], 16 * uses)

            @block.tensor
            def _(te):
                section("pe")

            @block.vector
            def _(dv):
                section("dve")

            @block.scalar
            def _(sc):
                section("act")

            @block.gpsimd
            def _(gp):
                section("pool")

            @block.sync
            def _(sy):
                section("sp")

    def stats(self):
        return {e: self.cnt[e] for e in self.COMPUTE} | {
            "dma": {q: sum(self.dma_uses[q]) for q in self.DMA_QUEUES}}


def build_k12():
    nc = bass.Bass("TRN2", target_bir_lowering=False)
    es = ExitStack()
    S = Sched(nc, es)

    T = 8192
    NT = 16384
    x_d = nc.dram_tensor("x", [NT, 1024], F32, kind="ExternalInput").ap()
    wqkvT_d = nc.dram_tensor("wqkvT", [1024, 1536], F32, kind="ExternalInput").ap()
    woT_d = nc.dram_tensor("woT", [512, 1024], F32, kind="ExternalInput").ap()
    peT_d = nc.dram_tensor("peT", [512, T], F32, kind="ExternalInput").ap()
    xn_out = nc.dram_tensor("xn_out", [NT, 1024], BF16, kind="ExternalOutput").ap()
    s0p_d = nc.dram_tensor("s0p", [NT, 1024], F32, kind="ExternalOutput").ap()

    qkT_sp = nc.dram_tensor("qkT_sp", [2, 1024, T], BF16, kind="Internal").ap()
    v_sp = nc.dram_tensor("v_sp", [2, T, 512], BF16, kind="Internal").ap()

    masks_np = np.zeros((128, 4 * 512), dtype=bfd)
    jj = np.arange(128)[:, None]
    ii = np.arange(512)[None, :]
    for o in range(4):
        masks_np[:, o * 512:(o + 1) * 512] = (ii >= jj + 128 * o).astype(bfd)
    mask_h = nc.inline_tensor(np.ascontiguousarray(masks_np), "maskc")
    ones_h = nc.inline_tensor(np.ones((128, 1), dtype=bfd), "onescol")
    onesr_h = nc.inline_tensor(np.ones((1, 128), dtype=np.float32), "onesrow")

    es12 = ExitStack()
    sb12 = lambda name, shape, dt: es12.enter_context(nc.sbuf_tensor(name, shape, dt))
    sbt = lambda name, shape, dt: es.enter_context(nc.sbuf_tensor(name, shape, dt))
    pst = lambda name, shape: es.enter_context(nc.psum_tensor(name, shape, F32))

    x_tiles = [sb12(f"x{i}", [128, 1024], F32) for i in range(4)]
    eps_sb = sb12("eps_sb", [128, 1], F32)
    stat = sb12("stat", [128, 8], F32)
    sq_scr = sb12("sq_scr", [128, 1024], F32)
    xn_tiles = [sb12(f"xn{i}", [128, 1024], BF16) for i in range(4)]
    w_sb = [sb12(f"w{e}", [128, 1536], BF16) for e in range(8)]
    pe_sb = [sb12(f"pe{c}", [128, 2048], BF16) for c in range(4)]
    xnT_sb = [sb12(f"xnT{e}", [128, 2048], BF16) for e in range(8)]
    qk_ev = [sb12(f"qkev{i}", [128, 2048], BF16) for i in range(2)]
    v_ev = [sb12(f"vev{i}", [128, 512], BF16) for i in range(4)]

    scr2 = pst("scr2", [128, 1024])
    oT_ps = pst("oT_ps", [128, 2048])
    den_ps = pst("den_ps", [1, 512])
    bc_ps = pst("bc_ps", [128, 512])
    SCR = lambda h: scr2[:, h * 512:(h + 1) * 512]
    OTP = lambda c: oT_ps[:, c * 512:(c + 1) * 512]

    for e in range(8):
        S.dma("qp", w_sb[e][:, :], wqkvT_d[e * 128:(e + 1) * 128, :], writes=[("w", e)])

    S.op("dve", lambda e: e.memset(eps_sb[:, :], 1e-5), writes=[("eps",)])
    ntile = NT // 128
    for i in range(ntile):
        bi = i % 4
        xs = x_tiles[bi]
        xnb = xn_tiles[bi]
        xr = ("x", bi)
        S.dma("qs", xs[:, :], x_d[i * 128:(i + 1) * 128, :], writes=[xr])
        S.op("dve", lambda e, xs=xs: e.reduce_sum(stat[:, 0:1], xs[:, :], axis=AX.X),
             reads=[xr], writes=[("st0",)], hazard=True)
        S.op("act", lambda e, xs=xs: e.activation(out=sq_scr[:, :], in_=xs[:, :],
                                                  func=AF.Square, accum_out=stat[:, 1:2]),
             reads=[xr], writes=[("sq",), ("st1",)])
        S.op("act", lambda e: e.mul(stat[:, 2:3], stat[:, 0:1], 1.0 / 1024),
             reads=[("st0",)], writes=[("st2",)])
        S.op("act", lambda e: e.activation(out=stat[:, 3:4], in_=stat[:, 2:3], func=AF.Square),
             reads=[("st2",)], writes=[("st3",)])
        S.op("dve", lambda e: e.scalar_tensor_tensor(stat[:, 4:5], stat[:, 1:2], 1.0 / 1024,
                                                     stat[:, 3:4], ALU.mult, ALU.subtract),
             reads=[("st1",), ("st3",)], writes=[("st4",)])
        S.op("act", lambda e: e.activation(out=stat[:, 5:6], in_=stat[:, 4:5], func=AF.Ln,
                                           bias=eps_sb[:, 0:1], scale=1.0),
             reads=[("st4",), ("eps",)], writes=[("st5",)])
        S.op("act", lambda e: e.activation(out=stat[:, 6:7], in_=stat[:, 5:6], func=AF.Exp,
                                           scale=-0.5),
             reads=[("st5",)], writes=[("st6",)])
        S.op("dve", lambda e: e.tensor_scalar(stat[:, 7:8], stat[:, 2:3], stat[:, 6:7],
                                              -1.0, ALU.mult, ALU.mult),
             reads=[("st2",), ("st6",)], writes=[("st7",)])
        S.op("dve", lambda e, xs=xs, xnb=xnb: e.tensor_scalar(
                 xnb[:, :], xs[:, :], stat[:, 6:7], stat[:, 7:8], ALU.mult, ALU.add),
             reads=[xr, ("st6",), ("st7",)], writes=[("xn", bi)])
        S.dma("qa", xn_out[i * 128:(i + 1) * 128, :], xnb[:, :],
              reads=[("xn", bi)], writes=[("xnout", i)])

    for p in range(4):
        for c in range(4):
            S.dma("qp", pe_sb[c][:, :], peT_d[c * 128:(c + 1) * 128, p * 2048:(p + 1) * 2048],
                  writes=[("pe", c)])
        for b in range(2):
            row0 = b * T + p * 2048
            xnout_tiles = [("xnout", row0 // 128 + k) for k in range(16)]
            for e in range(8):
                S.dma_transpose("qa", xnT_sb[e][:, :],
                                xn_out[row0:row0 + 2048, e * 128:(e + 1) * 128],
                                reads=xnout_tiles, writes=[("xnT", e)])
            for m in range(8):
                ev_buf = qk_ev[m % 2]
                evr = [("qkev", m % 2, t) for t in range(4)]
                for tch in range(4):
                    pp = SCR(tch % 2)
                    psk = ("scr2", tch % 2)
                    for e in range(8):
                        S.op("pe", lambda en, pp=pp, e=e, m=m, tch=tch: en.matmul(
                                 pp, w_sb[e][:, m * 128:(m + 1) * 128],
                                 xnT_sb[e][:, tch * 512:(tch + 1) * 512],
                                 start=(e == 0), stop=(e == 7)),
                             reads=[("w", e), ("xnT", e)], writes=[psk])
                    S.op("dve", lambda en, pp=pp, m=m, tch=tch, ev_buf=ev_buf: en.tensor_tensor(
                             ev_buf[:, tch * 512:(tch + 1) * 512], pp,
                             pe_sb[m % 4][:, tch * 512:(tch + 1) * 512], ALU.add),
                         reads=[psk, ("pe", m % 4)], writes=[evr[tch]])
                S.dma("qp", qkT_sp[b, m * 128:(m + 1) * 128, p * 2048:(p + 1) * 2048],
                      ev_buf[:, :], reads=evr, writes=[("qksp", b, m, p)])
            for tt in range(16):
                pp = OTP(tt % 2)
                psk = ("oT", tt % 2)
                vb = v_ev[tt % 4]
                for e in range(8):
                    S.op("pe", lambda en, pp=pp, e=e, tt=tt: en.matmul(
                             pp, xnT_sb[e][:, tt * 128:(tt + 1) * 128],
                             w_sb[e][:, 1024:1536],
                             start=(e == 0), stop=(e == 7)),
                         reads=[("w", e), ("xnT", e)], writes=[psk])
                S.op("act", lambda en, pp=pp, vb=vb: en.copy(vb[:, :], pp),
                     reads=[psk], writes=[("vev", tt % 4)])
                S.dma("qp", v_sp[b, p * 2048 + tt * 128: p * 2048 + (tt + 1) * 128, :],
                      vb[:, :], reads=[("vev", tt % 4)], writes=[("vsp", b, p, tt)])

    S.barrier_all()
    es12.close()
    kT_sb = [sbt(f"kT{c}", [128, T], BF16) for c in range(4)]
    v_sb = sbt("v_sb", [128, 64 * 512], BF16)
    qg_sb = [sbt(f"qg{c}", [128, 512], BF16) for c in range(4)]
    pT_sb = [sbt(f"pT{i}", [128, 512], BF16) for i in range(2)]
    mask_sb = sbt("mask_sb", [128, 4 * 512], BF16)
    ones_sb = sbt("ones_sb", [128, 1], BF16)
    onesr_sb = sbt("onesr_sb", [1, 128], F32)
    den_sb = sbt("den_sb", [1, 512], F32)
    recip_sb = sbt("recip_sb", [128, 512], F32)
    oT_sb = [sbt(f"oT{c}", [128, 512], BF16) for c in range(4)]
    wo_sb = [sbt(f"wo{c}", [128, 1024], BF16) for c in range(4)]
    s0_ev = [sbt(f"s0ev{i}", [128, 1024], F32) for i in range(4)]
    S.dma("qp", mask_sb[:, :], mask_h.ap(), writes=[("mask",)])
    S.dma("qp", ones_sb[:, :], ones_h.ap(), writes=[("onec",)])
    S.dma("qp", onesr_sb[:, :], onesr_h.ap(), writes=[("oner",)])
    for c in range(4):
        S.dma("qp", wo_sb[c][:, :], woT_d[c * 128:(c + 1) * 128, :], writes=[("wo", c)])

    for b in range(2):
        kspills = [("qksp", b, m, p) for m in range(4, 8) for p in range(4)]
        qspills = [("qksp", b, m, p) for m in range(4) for p in range(4)]
        for c in range(4):
            S.dma("qs", kT_sb[c][:, :], qkT_sp[b, 512 + c * 128: 512 + (c + 1) * 128, :],
                  reads=kspills, writes=[("kT", c)])
        for part in range(8):
            src = v_sp[b, part * 1024:(part + 1) * 1024, :]
            vsp_keys = [("vsp", b, (part * 1024 + k * 128) // 2048,
                         ((part * 1024 + k * 128) % 2048) // 128) for k in range(8)]
            S.dma("qp", v_sb[:, part * 4096:(part + 1) * 4096].rearrange(
                      "p (c d) -> p c d", c=8),
                  src.rearrange("(c p) d -> c p d", p=128).transpose([1, 0, 2]),
                  reads=vsp_keys, writes=[("vsb", part)])
        for g in range(16):
            for c in range(4):
                S.dma("qs", qg_sb[c][:, :],
                      qkT_sp[b, c * 128:(c + 1) * 128, g * 512:(g + 1) * 512],
                      reads=qspills, writes=[("qg", c)])
            nchunk = 4 * g + 4
            for ci in range(nchunk):
                pbuf = ci % 2
                st_ps = SCR(pbuf)
                stk = ("scr2", pbuf)
                pT = pT_sb[pbuf]
                ptk = ("pT", pbuf)
                first = (ci == 0)
                last = (ci == nchunk - 1)
                for d in range(4):
                    S.op("pe", lambda en, st_ps=st_ps, d=d, ci=ci: en.matmul(
                             st_ps, kT_sb[d][:, ci * 128:(ci + 1) * 128],
                             qg_sb[d][:, :], start=(d == 0), stop=(d == 3)),
                         reads=[("kT", d), ("qg", d)], writes=[stk])
                S.op("act", lambda en, st_ps=st_ps, pT=pT: en.activation(
                         out=pT[:, :], in_=st_ps, func=AF.Exp, scale=RSQRT512),
                     reads=[stk], writes=[ptk])
                off = ci - 4 * g
                if off >= 0:
                    S.op("dve", lambda en, pT=pT, off=off: en.tensor_tensor(
                             pT[:, :], pT[:, :], mask_sb[:, off * 512:(off + 1) * 512],
                             ALU.mult),
                         reads=[("mask",)], writes=[ptk])
                S.op("pe", lambda en, pT=pT, first=first, last=last: en.matmul(
                         den_ps[:, :], ones_sb[:, :], pT[:, :],
                         start=first, stop=last),
                     reads=[("onec",), ptk], writes=[("denps",)])
                for dv in range(4):
                    S.op("pe", lambda en, dv=dv, ci=ci, pT=pT, first=first, last=last: en.matmul(
                             OTP(dv), v_sb[:, ci * 512 + dv * 128: ci * 512 + (dv + 1) * 128],
                             pT[:, :], start=first, stop=last),
                         reads=[("vsb", ci // 8), ptk], writes=[("oT", dv)])
            S.op("dve", lambda en: en.tensor_copy(den_sb[:, :], den_ps[:, :]),
                 reads=[("denps",)], writes=[("densb",)])
            S.op("pe", lambda en: en.matmul(bc_ps[:, :], onesr_sb[:, :], den_sb[:, :],
                                            start=True, stop=True),
                 reads=[("oner",), ("densb",)], writes=[("bcps",)])
            S.op("act", lambda en: en.activation(out=recip_sb[:, :], in_=bc_ps[:, :],
                                                 func=AF.Ln),
                 reads=[("bcps",)], writes=[("recip",)])
            S.op("act", lambda en: en.activation(out=recip_sb[:, :], in_=recip_sb[:, :],
                                                 func=AF.Exp, scale=-1.0),
                 writes=[("recip",)])
            for dv in range(4):
                S.op("dve", lambda en, dv=dv: en.tensor_tensor(
                         oT_sb[dv][:, :], OTP(dv), recip_sb[:, :], ALU.mult),
                     reads=[("oT", dv), ("recip",)], writes=[("oTsb", dv)])
            for tt in range(4):
                sev = s0_ev[tt]
                for half in range(2):
                    pp = SCR(half)
                    psk = ("scr2", half)
                    for dv in range(4):
                        S.op("pe", lambda en, pp=pp, dv=dv, tt=tt, half=half: en.matmul(
                                 pp, oT_sb[dv][:, tt * 128:(tt + 1) * 128],
                                 wo_sb[dv][:, half * 512:(half + 1) * 512],
                                 start=(dv == 0), stop=(dv == 3)),
                             reads=[("oTsb", dv), ("wo", dv)], writes=[psk])
                    S.op("act", lambda en, pp=pp, sev=sev, half=half: en.copy(
                             sev[:, half * 512:(half + 1) * 512], pp),
                         reads=[psk], writes=[("s0ev", tt, half)])
                S.dma("qp", s0p_d[b * T + g * 512 + tt * 128:
                                  b * T + g * 512 + (tt + 1) * 128, :],
                      sev[:, :], reads=[("s0ev", tt, 0), ("s0ev", tt, 1)])

    stats = S.stats()
    S.finalize()
    es.close()
    return nc, stats


def pe_table():
    T, D = 8192, 512
    pos = np.arange(T, dtype=np.float64)[:, None]
    div = np.exp(np.arange(0, D, 2, dtype=np.float64) * (-np.log(10000.0) / D))
    ang = pos * div
    pe = np.stack([np.sin(ang), np.cos(ang)], axis=-1).reshape(T, D)
    return np.ascontiguousarray(pe.T.astype(np.float32))


def k12_inputs(x, Wqkv, Wo):
    xf = np.ascontiguousarray(x.reshape(16384, 1024).astype(np.float32))
    peT = pe_table()
    maps = []
    for h in range(8):
        wq = np.ascontiguousarray(Wqkv[h * 1536:(h + 1) * 1536, :].T.astype(np.float32))
        wo = np.ascontiguousarray(Wo[:, h * 512:(h + 1) * 512].T.astype(np.float32))
        maps.append({"x": xf, "wqkvT": wq, "woT": wo, "peT": peT})
    return maps


HIN = 2176
NQKV = 16
NTAU = 12
NWIN = 3
K3_SKIP_B = False
K3_SKIP_A = False
K3A_NO_ATT = False
K3A_MM_ONLY = False
K3A_NO_XA = False
K3B_STAGE = 9
K3_TAP_XA = False
K3_DBG = False

def _ln_stats(S, stat, eps_sb, src_ap, src_reads, scr_ap, pref, scr_keys=()):
    F = src_ap.shape[-1]
    S.op("dve", lambda e: e.reduce_sum(stat[:, 0:1], src_ap, axis=AX.X),
         reads=src_reads, writes=[(pref, 0)], hazard=True)
    S.op("act", lambda e: e.activation(out=scr_ap, in_=src_ap,
                                       func=AF.Square, accum_out=stat[:, 1:2]),
         reads=src_reads, writes=[(pref, "scr"), (pref, 1)] + list(scr_keys))
    S.op("act", lambda e: e.mul(stat[:, 2:3], stat[:, 0:1], 1.0 / F),
         reads=[(pref, 0)], writes=[(pref, 2)])
    S.op("act", lambda e: e.activation(out=stat[:, 3:4], in_=stat[:, 2:3], func=AF.Square),
         reads=[(pref, 2)], writes=[(pref, 3)])
    S.op("dve", lambda e: e.scalar_tensor_tensor(stat[:, 4:5], stat[:, 1:2], 1.0 / F,
                                                 stat[:, 3:4], ALU.mult, ALU.subtract),
         reads=[(pref, 1), (pref, 3)], writes=[(pref, 4)])
    S.op("act", lambda e: e.activation(out=stat[:, 5:6], in_=stat[:, 4:5], func=AF.Ln,
                                       bias=eps_sb[:, 0:1], scale=1.0),
         reads=[(pref, 4), ("eps",)], writes=[(pref, 5)])
    S.op("act", lambda e: e.activation(out=stat[:, 6:7], in_=stat[:, 5:6], func=AF.Exp,
                                       scale=-0.5),
         reads=[(pref, 5)], writes=[(pref, 6)])
    S.op("dve", lambda e: e.tensor_scalar(stat[:, 7:8], stat[:, 2:3], stat[:, 6:7],
                                          -1.0, ALU.mult, ALU.mult),
         reads=[(pref, 2), (pref, 6)], writes=[(pref, 7)])


def build_k3(layer):
    nc = bass.Bass("TRN2", target_bir_lowering=False)
    es = ExitStack()
    S = Sched(nc, es)
    L = layer

    xin_bf = nc.dram_tensor("xin_bf", [2, HIN, 1024], BF16, kind="ExternalInput").ap()
    xin_f = (nc.dram_tensor("xin_f", [2, HIN, 1024], F32, kind="ExternalInput").ap()
             if L == 1 else None)
    wqkv_d = nc.dram_tensor("wqkv", [1024, 1536], F32, kind="ExternalInput").ap()
    qs_d = nc.dram_tensor("qs_row", [1, 512], F32, kind="ExternalInput").ap()
    ks_d = nc.dram_tensor("ks_row", [1, 512], F32, kind="ExternalInput").ap()
    rot_d = nc.dram_tensor("rot", [4, 2048, 512], F32, kind="ExternalInput").ap()
    pw_d = nc.dram_tensor("pw", [128, NWIN], F32, kind="ExternalInput").ap()
    wo_d = nc.dram_tensor("wo", [512, 1024], F32, kind="ExternalInput").ap()
    lg_d = nc.dram_tensor("lg_col", [128, 8], F32, kind="ExternalInput").ap()
    lb_d = nc.dram_tensor("lb_col", [128, 8], F32, kind="ExternalInput").ap()
    w1_d = nc.dram_tensor("w1", [1024, 6144], F32, kind="ExternalInput").ap()
    b1_d = nc.dram_tensor("b1_row", [1, 6144], F32, kind="ExternalInput").ap()
    w2_d = nc.dram_tensor("w2", [3072, 1024], F32, kind="ExternalInput").ap()
    b2_d = nc.dram_tensor("b2_row", [1, 1024], F32, kind="ExternalInput").ap()
    if L == 0:
        FT, FR0 = 8, 1152
        out_f = nc.dram_tensor("out_f", [2, FT * 128, 1024], F32, kind="ExternalOutput").ap()
        out_b = nc.dram_tensor("out_b", [2, FT * 128, 1024], BF16, kind="ExternalOutput").ap()
    else:
        FT, FR0 = 9, 1024
        s0p_d = nc.dram_tensor("s0p_in", [8, 2, FT * 128, 1024], F32, kind="ExternalInput").ap()
        x0_d = nc.dram_tensor("x0_in", [2, FT * 128, 1024], F32, kind="ExternalInput").ap()
        out_f = nc.dram_tensor("out_f", [2, FT * 128, 1024], F32, kind="ExternalOutput").ap()
        out_b = None
    if K3_DBG:
        dbg_d = nc.dram_tensor("dbg", [128, 2576], F32, kind="ExternalOutput").ap()
    xa_sp = nc.dram_tensor("xa_sp", [2, NTAU * 128 + 1, 1024], F32,
                           kind=("ExternalOutput" if K3_TAP_XA else "Internal")).ap()

    jj = np.arange(128)[:, None]
    ii = np.arange(512)[None, :]
    lm = np.zeros((128, 8 * 512), dtype=bfd)
    for kc in range(4):
        lm[:, kc * 512:(kc + 1) * 512] = (ii <= jj + 128 * kc).astype(bfd)
        lm[:, (4 + kc) * 512:(5 + kc) * 512] = (ii >= jj + 128 * kc).astype(bfd)
    lmask_h = nc.inline_tensor(np.ascontiguousarray(lm), "lmask")
    onesc_h = nc.inline_tensor(np.ones((128, 1), dtype=bfd), "onescol")
    onesrb_h = nc.inline_tensor(np.ones((1, 128), dtype=bfd), "onesrowb")
    onesrf_h = nc.inline_tensor(np.ones((1, 128), dtype=np.float32), "onesrowf")
    ident_h = nc.inline_tensor(np.eye(128, dtype=bfd), "identb")

    sbt = lambda name, shape, dt: es.enter_context(nc.sbuf_tensor(name, shape, dt))
    pst = lambda name, shape, dt=F32: es.enter_context(nc.psum_tensor(name, shape, dt))

    eps_sb = sbt("eps_sb", [128, 1], F32)
    tiny_sb = sbt("tiny_sb", [128, 1], F32)
    stat = sbt("stat", [128, 16], F32)
    onesc_sb = sbt("onesc_sb", [128, 1], BF16)
    onesrf_sb = sbt("onesrf_sb", [1, 128], F32)
    ident_sb = sbt("ident_sb", [128, 128], BF16)
    S.op("dve", lambda e: e.memset(eps_sb[:, :], 1e-5), writes=[("eps",)])
    S.op("dve", lambda e: e.memset(tiny_sb[:, :], 1e-38), writes=[("tiny",)])
    S.dma("qp", onesc_sb[:, :], onesc_h.ap(), writes=[("onec",)])
    S.dma("qp", onesrf_sb[:, :], onesrf_h.ap(), writes=[("onerf",)])
    S.dma("qp", ident_sb[:, :], ident_h.ap(), writes=[("ident",)])

    ppA = pst("ppA", [128, 1024])
    tp_ps = pst("tp_ps", [128, 1024], BF16)
    st_ps = pst("st_ps", [128, 1024])
    den_ps = pst("den_ps", [1, 512])
    oT_ps = pst("oT_ps", [128, 512])
    bc_ps = pst("bc_ps", [128, 512])
    PA = lambda h: ppA[:, h * 512:(h + 1) * 512]

    esA = ExitStack()
    sbA = lambda name, shape, dt: esA.enter_context(nc.sbuf_tensor(name, shape, dt))
    w_sb = [sbA(f"wq{e}", [128, 1536], BF16) for e in range(8)]
    wo_sb = [sbA(f"wo{c}", [128, 1024], BF16) for c in range(4)]
    qs_sb = sbA("qs_sb", [1, 512], F32)
    ks_sb = sbA("ks_sb", [1, 512], F32)
    QS_sb = sbA("QS_sb", [128, 512], F32)
    KS_sb = sbA("KS_sb", [128, 512], F32)
    pw_sb = sbA("pw_sb", [128, NWIN], F32)
    xnT_sb = [sbA(f"xnT{e}", [128, 2048], BF16) for e in range(8)]
    qkv_f = [sbA(f"qkvf{i}", [128, 1536], F32) for i in range(2)]
    rot_sb = [sbA(f"rot{i}", [128, 4 * 512], BF16) for i in range(2)]
    scr5 = sbA("scr5", [128, 512], F32)
    scr5b = sbA("scr5b", [128, 512], F32)
    scr5c = sbA("scr5c", [128, 512], F32)
    qp_sb = sbA("qp_sb", [128, 512], BF16)
    kp_sb = sbA("kp_sb", [128, 512], BF16)
    kT_sb = [sbA(f"kTl{c}", [128, 2048], BF16) for c in range(8)]
    qT_sb = [sbA(f"qTl{c}", [128, NTAU * 128], BF16) for c in range(8)]
    pT2_sb = [sbA(f"pT2{i}", [128, 512], BF16) for i in range(2)]
    v_sb = sbA("v_sbl", [128, 16 * 512 + 64], BF16)
    lmask_sb = sbA("lmask_sb", [128, 8 * 512], BF16)
    denr_sb = sbA("denr_sb", [1, 512], F32)
    recl_sb = sbA("recl_sb", [64, 512], F32)
    oT_all = [sbA(f"oTall{c}", [128, NTAU * 128], BF16) for c in range(4)]
    xn_tok = sbA("xn_tok", [128, 1024], F32 if L == 1 else BF16)
    xa_ev = [sbA("xaev0", [128, 1024], F32)] * 2

    for hh in range(8):
        S.op("dve", lambda e, hh=hh: e.memset(kT_sb[hh][:, :], 0.0),
             writes=[("kTz", hh)])
        S.op("dve", lambda e, hh=hh: e.memset(qT_sb[hh][:, :], 0.0),
             writes=[("qTz", hh)])
    S.dma("qp", lmask_sb[:, :], lmask_h.ap(), writes=[("lmask",)])
    S.dma("qp", pw_sb[:, :], pw_d, writes=[("pw",)])
    for e in range(8):
        S.dma("qp", w_sb[e][:, :], wqkv_d[e * 128:(e + 1) * 128, :], writes=[("w", e)])
    for c in range(4):
        S.dma("qp", wo_sb[c][:, :], wo_d[c * 128:(c + 1) * 128, :], writes=[("wo", c)])
    S.dma("qs", qs_sb[:, :], qs_d, writes=[("qsr",)])
    S.dma("qs", ks_sb[:, :], ks_d, writes=[("ksr",)])
    S.op("pe", lambda e: e.matmul(PA(0), onesrf_sb[:, :], qs_sb[:, :], start=True, stop=True),
         reads=[("onerf",), ("qsr",)], writes=[("ppA", 0)])
    S.op("dve", lambda e: e.tensor_copy(QS_sb[:, :], PA(0)),
         reads=[("ppA", 0)], writes=[("QS",)])
    S.op("pe", lambda e: e.matmul(PA(1), onesrf_sb[:, :], ks_sb[:, :], start=True, stop=True),
         reads=[("onerf",), ("ksr",)], writes=[("ppA", 1)])
    S.op("dve", lambda e: e.tensor_copy(KS_sb[:, :], PA(1)),
         reads=[("ppA", 1)], writes=[("KS",)])

    for b in range(0 if not K3_SKIP_A else 2, 2):
        for e in range(8):
            r0 = 128 if e < 4 else 127
            S.dma_transpose("qa", xnT_sb[e][:, :],
                            xin_bf[b, r0:r0 + 2048, e * 128:(e + 1) * 128],
                            writes=[("xnT", e)])
        for j in range(NQKV):
            qkv = qkv_f[j % 2]
            qkr = lambda c, j=j: ("qkvf", j % 2, c)
            for cc in range(3):
                for e in range(8):
                    S.op("pe", lambda en, e=e, j=j, cc=cc: en.matmul(
                             PA(cc % 2), xnT_sb[e][:, j * 128:(j + 1) * 128],
                             w_sb[e][:, cc * 512:(cc + 1) * 512],
                             start=(e == 0), stop=(e == 7)),
                         reads=[("w", e), ("xnT", e)], writes=[("ppA", cc % 2)])
                S.op("act", lambda en, qkv=qkv, cc=cc: en.copy(
                         qkv[:, cc * 512:(cc + 1) * 512], PA(cc % 2)),
                     reads=[("ppA", cc % 2)], writes=[qkr(cc)])
            S.op("act", lambda en, qkv=qkv, j=j: en.copy(
                     v_sb[:, j * 512:(j + 1) * 512], qkv[:, 1024:1536]),
                 reads=[qkr(2)], writes=[("vtk", j)])
            if K3A_MM_ONLY:
                continue
            rt = rot_sb[j % 2]
            for t4 in range(4):
                S.dma("qp", rt[:, t4 * 512:(t4 + 1) * 512],
                      rot_d[t4, j * 128:(j + 1) * 128, :], writes=[("rot", j % 2, t4)])
            for qk in range(2):
                col0 = qk * 512
                SC, sck = (QS_sb, ("QS",)) if qk == 0 else (KS_sb, ("KS",))
                dst, dsk = (qp_sb, ("qp",)) if qk == 0 else (kp_sb, ("kp",))
                A_t = rt[:, (2 * qk) * 512:(2 * qk + 1) * 512]
                B_t = rt[:, (2 * qk + 1) * 512:(2 * qk + 2) * 512]
                Ak = ("rot", j % 2, 2 * qk)
                Bk = ("rot", j % 2, 2 * qk + 1)
                S.op("act", lambda en, qkv=qkv, col0=col0: en.activation(
                         out=scr5[:, :], in_=qkv[:, col0:col0 + 512], func=AF.Square),
                     reads=[qkr(qk)], writes=[("scr5",)])
                S.op("dve", lambda en: en.reduce_sum(
                         stat[:, 8:16], scr5[:, :].rearrange("p (h d) -> p h d", h=8),
                         axis=AX.X),
                     reads=[("scr5",)], writes=[("nrm",)], hazard=True)
                S.op("act", lambda en: en.activation(out=stat[:, 8:16], in_=stat[:, 8:16],
                                                     func=AF.Ln, bias=tiny_sb[:, 0:1]),
                     reads=[("tiny",)], writes=[("nrm",)])
                S.op("act", lambda en: en.activation(out=stat[:, 8:16], in_=stat[:, 8:16],
                                                     func=AF.Exp, scale=-0.5),
                     writes=[("nrm",)])
                for h in range(8):
                    S.op("dve", lambda en, qkv=qkv, col0=col0, h=h: en.tensor_scalar(
                             scr5[:, h * 64:(h + 1) * 64],
                             qkv[:, col0 + h * 64:col0 + (h + 1) * 64],
                             stat[:, 8 + h:9 + h], None, ALU.mult),
                         reads=[qkr(qk), ("nrm",)], writes=[("scr5",)])
                S.op("dve", lambda en, SC=SC: en.tensor_tensor(
                         scr5[:, :], scr5[:, :], SC[:, :], ALU.mult),
                     reads=[sck], writes=[("scr5",)])
                S.op("dve", lambda en, A_t=A_t: en.tensor_tensor(
                         scr5b[:, :], scr5[:, :], A_t, ALU.mult),
                     reads=[("scr5",), Ak], writes=[("scr5b",)])
                q3v = scr5[:, :].rearrange("p (h d) -> p h d", h=8)
                Bv = B_t.rearrange("p (h d) -> p h d", h=8)
                mv = scr5c[:, :].rearrange("p (h d) -> p h d", h=8)
                S.op("dve", lambda en, q3v=q3v, Bv=Bv, mv=mv: en.tensor_tensor(
                         mv[:, :, 0:32], q3v[:, :, 32:64], Bv[:, :, 0:32], ALU.mult),
                     reads=[("scr5",), Bk], writes=[("scr5c",)])
                S.op("dve", lambda en, q3v=q3v, Bv=Bv, mv=mv: en.tensor_tensor(
                         mv[:, :, 32:64], q3v[:, :, 0:32], Bv[:, :, 32:64], ALU.mult),
                     reads=[("scr5",), Bk], writes=[("scr5c",)])
                S.op("dve", lambda en, dst=dst: en.tensor_tensor(
                         dst[:, :], scr5b[:, :], scr5c[:, :], ALU.add),
                     reads=[("scr5b",), ("scr5c",)], writes=[dsk])
            if K3_DBG and b == 0 and j == 8:
                S.dma("qp", dbg_d[:, 0:1536], qkv[:, :], reads=[qkr(0), qkr(1), qkr(2)])
                S.dma("qp", dbg_d[:, 1536:2048], qp_sb[:, :], reads=[("qp",)])
                S.dma("qp", dbg_d[:, 2048:2560], kp_sb[:, :], reads=[("kp",)])
                S.dma("qp", dbg_d[:, 2560:2576], stat[:, 0:16], reads=[("nrm",)])
            for qk in range(2):
                src, srck = (qp_sb, ("qp",)) if qk == 0 else (kp_sb, ("kp",))
                if qk == 0 and j < 4:
                    continue
                for c in range(4):
                    slot = tp_ps[:, c * 256:c * 256 + 128]
                    S.op("pe", lambda en, src=src, c=c, slot=slot: en.transpose(
                             slot, src[:, c * 128:(c + 1) * 128], ident_sb[:, :]),
                         reads=[srck, ("ident",)], writes=[("tp",)])
                    for sub in range(2):
                        hh = 2 * c + sub
                        if qk == 1:
                            S.op("act", lambda en, hh=hh, j=j, slot=slot, sub=sub: en.copy(
                                     kT_sb[hh][0:64, j * 128:(j + 1) * 128],
                                     slot[sub * 64:(sub + 1) * 64, :]),
                                 reads=[("tp",), ("kTz", hh)], writes=[("kT", hh, j)])
                        else:
                            S.op("act", lambda en, hh=hh, j=j, slot=slot, sub=sub: en.copy(
                                     qT_sb[hh][0:64, (j - 4) * 128:(j - 3) * 128],
                                     slot[sub * 64:(sub + 1) * 64, :]),
                                 reads=[("tp",), ("qTz", hh)], writes=[("qT", hh, j - 4)])
        for w in range(1, 4) if not (K3A_NO_ATT or K3A_MM_ONLY) else []:
            qcol0 = (w - 1) * 512
            for h in range(8):
                qreads = [("qT", h, 4 * (w - 1) + t) for t in range(4)]
                for kc in range(8):
                    krow = (w - 1) * 512 + kc * 128
                    jt = krow // 128
                    pb = kc % 2
                    stk = ("stps", pb)
                    pT = pT2_sb[pb]
                    ptk = ("pT2", pb)
                    S.op("pe", lambda en, h=h, krow=krow, pb=pb, qcol0=qcol0: en.matmul(
                             st_ps[:, pb * 512:(pb + 1) * 512],
                             kT_sb[h][:, krow:krow + 128],
                             qT_sb[h][:, qcol0:qcol0 + 512],
                             start=True, stop=True),
                         reads=[("kT", h, jt), ("kTz", h), ("qTz", h)] + qreads,
                         writes=[stk])
                    S.op("act", lambda en, pb=pb, pT=pT: en.activation(
                             out=pT[:, :], in_=st_ps[:, pb * 512:(pb + 1) * 512],
                             func=AF.Exp, scale=8.0),
                         reads=[stk], writes=[ptk])
                    S.op("dve", lambda en, pT=pT, kc=kc: en.tensor_tensor(
                             pT[:, :], pT[:, :], lmask_sb[:, kc * 512:(kc + 1) * 512],
                             ALU.mult),
                         reads=[("lmask",)], writes=[ptk])
                    if kc < 4:
                        S.op("dve", lambda en, pT=pT, w=w: en.tensor_scalar(
                                 pT[:, :], pT[:, :], pw_sb[:, w - 1:w], None, ALU.mult),
                             reads=[("pw",)], writes=[ptk])
                    S.op("pe", lambda en, pT=pT, kc=kc: en.matmul(
                             den_ps[:, :], onesc_sb[:, :], pT[:, :],
                             start=(kc == 0), stop=(kc == 7)),
                         reads=[("onec",), ptk], writes=[("denps",)])
                    S.op("pe", lambda en, pT=pT, kc=kc, jt=jt, h=h: en.matmul(
                             oT_ps[:, :],
                             v_sb[:, jt * 512 + h * 64:jt * 512 + h * 64 + 128],
                             pT[:, :], start=(kc == 0), stop=(kc == 7)),
                         reads=[("vtk", jt), ptk], writes=[("oTps",)])
                S.op("dve", lambda en: en.tensor_copy(denr_sb[:, :], den_ps[:, :]),
                     reads=[("denps",)], writes=[("denr",)])
                S.op("act", lambda en: en.activation(out=denr_sb[:, :], in_=denr_sb[:, :],
                                                     func=AF.Ln),
                     writes=[("denr",)])
                S.op("act", lambda en: en.activation(out=denr_sb[:, :], in_=denr_sb[:, :],
                                                     func=AF.Exp, scale=-1.0),
                     writes=[("denr",)])
                S.op("pe", lambda en: en.matmul(bc_ps[:, :], onesrf_sb[:, :],
                                                denr_sb[:, :], start=True, stop=True),
                     reads=[("onerf",), ("denr",)], writes=[("bcps",)])
                S.op("dve", lambda en: en.tensor_copy(recl_sb[:, :], bc_ps[0:64, :]),
                     reads=[("bcps",)], writes=[("recl",)])
                S.op("dve", lambda en, h=h, qcol0=qcol0: en.tensor_tensor(
                         oT_all[h // 2][(h % 2) * 64:(h % 2) * 64 + 64, qcol0:qcol0 + 512],
                         oT_ps[0:64, :], recl_sb[:, :], ALU.mult),
                     reads=[("oTps",), ("recl",)],
                     writes=[("oTa", h // 2, (h % 2) * 64, w)])
        for tau in range(NTAU) if not (K3A_NO_ATT or K3A_MM_ONLY or K3A_NO_XA) else []:
            inrow = 640 + tau * 128
            if L == 0:
                S.dma("qs", xn_tok[:, :], xin_bf[b, inrow:inrow + 128, :],
                      writes=[("xntok",)])
            else:
                S.dma("qs", xn_tok[:, :], xin_f[b, inrow:inrow + 128, :],
                      writes=[("xntok",)])
            wq = tau // 4 + 1
            oreads = [("oTa", ch2, po2, wq) for ch2 in range(4) for po2 in (0, 64)]
            xa = xa_ev[tau % 2]
            for half in range(2):
                for c in range(4):
                    S.op("pe", lambda en, c=c, half=half, tau=tau: en.matmul(
                             PA(half), oT_all[c][:, tau * 128:(tau + 1) * 128],
                             wo_sb[c][:, half * 512:(half + 1) * 512],
                             start=(c == 0), stop=(c == 3)),
                         reads=oreads + [("wo", c)], writes=[("ppA", half)])
                S.op("dve", lambda en, xa=xa, half=half: en.tensor_tensor(
                         xa[:, half * 512:(half + 1) * 512], PA(half),
                         xn_tok[:, half * 512:(half + 1) * 512], ALU.add),
                     reads=[("ppA", half), ("xntok",)], writes=[("xaev", half)])
            S.dma("qp", xa_sp[b, 1 + tau * 128:1 + (tau + 1) * 128, :], xa[:, :],
                  reads=[("xaev", 0), ("xaev", 1)],
                  writes=[("xasp", b, tau)])

    S.barrier_all()
    esA.close()
    esB = ExitStack()
    sbB = lambda name, shape, dt: esB.enter_context(nc.sbuf_tensor(name, shape, dt))
    w1_sb = [sbB(f"w1s{e}", [128, 6144], BF16) for e in range(8)]
    w2_sb = [sbB(f"w2s{c}", [128, 1024], BF16) for c in range(24)]
    b1_sb = sbB("b1_sb", [1, 6144], BF16)
    b2_sb = sbB("b2_sb", [1, 1024], BF16)
    onesrb_sb = sbB("onesrb_sb", [1, 128], BF16)
    lg_sb = sbB("lg_sb", [128, 8], F32)
    lb_sb = sbB("lb_sb", [128, 8], F32)
    hp_sb = sbB("hp_sb", [128, 1024], F32)
    xa_t = sbB("xa_t", [128, 1024], F32)
    hbf_sb = sbB("hbf_sb", [128, 1024], BF16)
    hT_sb = sbB("hT_sb", [128, 8 * 128], BF16)
    ascr_sb = sbB("ascr_sb", [128, 512], F32)
    gl_sb = sbB("gl_sb", [128, 512], F32)
    h2_sb = sbB("h2_sb", [128, 3072], BF16)
    h2T_sb = sbB("h2T_sb", [128, 24 * 128], BF16)
    oev = sbB("oev", [128, 1024], F32)
    if L == 1:
        g0_sb = sbB("g0_sb", [128, 1024], BF16)
        s0a_sb = None
        s0l_sb = [hp_sb, oev]
        s0l_keys = [[("hp", 0), ("hp", 1)], [("oev", 0), ("oev", 1)]]
        x0_sb = xa_t
        obf_sb = None
    else:
        obf_sb = sbB("obf_sb", [128, 1024], BF16)

    for e in range(8):
        S.dma("qp", w1_sb[e][:, :], w1_d[e * 128:(e + 1) * 128, :], writes=[("w1", e)])
    for c in range(24):
        S.dma("qp", w2_sb[c][:, :], w2_d[c * 128:(c + 1) * 128, :], writes=[("w2", c)])
    S.dma("qp", b1_sb[:, :], b1_d, writes=[("b1",)])
    S.dma("qp", b2_sb[:, :], b2_d, writes=[("b2",)])
    S.dma("qp", onesrb_sb[:, :], onesrb_h.ap(), writes=[("onerb",)])
    S.dma("qs", lg_sb[:, :], lg_d, writes=[("lgc",)])
    S.dma("qs", lb_sb[:, :], lb_d, writes=[("lbc",)])

    ACOLS = [(i * 512, (i + 1) * 512) for i in range(6)]
    for b in range(0 if not K3_SKIP_B else 2, 2):
        for tf in range(FT):
            r0 = FR0 + 128 * tf
            xi = r0 - 639
            ta = (r0 - 640) // 128
            S.dma("qs", hp_sb[:, 0:512], xa_sp[b, xi:xi + 128, 0:512],
                  reads=[("xasp", b, ta)], writes=[("hp", 0)])
            S.dma("qs", hp_sb[:, 512:1024], xa_sp[b, xi - 1:xi + 127, 512:1024],
                  reads=[("xasp", b, ta), ("xasp", b, ta - 1)] if ta >= 1 else [("xasp", b, ta)],
                  writes=[("hp", 1)])
            S.dma("qs", xa_t[:, :], xa_sp[b, xi:xi + 128, :],
                  reads=[("xasp", b, ta)], writes=[("xat",)])
            _ln_stats(S, stat, eps_sb, hp_sb[:, :], [("hp", 0), ("hp", 1)], oev[:, :], "ls",
                      scr_keys=[("oev", 0), ("oev", 1)])
            S.op("dve", lambda e: e.tensor_scalar(hbf_sb[:, :], hp_sb[:, :], stat[:, 6:7],
                                                  stat[:, 7:8], ALU.mult, ALU.add),
                 reads=[("hp", 0), ("hp", 1), ("ls", 6), ("ls", 7)], writes=[("hbf",)])
            if K3B_STAGE < 2:
                continue
            for c in range(8):
                slot = tp_ps[:, (c % 8) * 128:(c % 8) * 128 + 128]
                S.op("pe", lambda en, c=c, slot=slot: en.transpose(
                         slot, hbf_sb[:, c * 128:(c + 1) * 128], ident_sb[:, :]),
                     reads=[("hbf",), ("ident",)], writes=[("tp",)])
                S.op("dve", lambda en, c=c, slot=slot: en.tensor_scalar(
                         hT_sb[:, c * 128:(c + 1) * 128], slot,
                         lg_sb[:, c:c + 1], lb_sb[:, c:c + 1], ALU.mult, ALU.add),
                     reads=[("tp",), ("lgc",), ("lbc",)], writes=[("hT", c)])
            hreads = [("hT", c) for c in range(8)]
            if K3B_STAGE < 3:
                continue
            for ci, (c0, c1) in enumerate(ACOLS):
                for part in range(2):
                    off = 0 if part == 0 else 3072
                    pp = PA(part)
                    for e in range(8):
                        S.op("pe", lambda en, e=e, c0=c0, off=off, pp=pp: en.matmul(
                                 pp, hT_sb[:, e * 128:(e + 1) * 128],
                                 w1_sb[e][:, off + c0:off + c0 + 512],
                                 start=(e == 0), stop=False),
                             reads=hreads + [("w1", e)], writes=[("ppA", part)])
                    S.op("pe", lambda en, c0=c0, off=off, pp=pp: en.matmul(
                             pp, onesrb_sb[:, :], b1_sb[:, off + c0:off + c0 + 512],
                             start=False, stop=True),
                         reads=[("onerb",), ("b1",)], writes=[("ppA", part)])
                S.op("act", lambda en: en.copy(ascr_sb[:, :], PA(0)),
                     reads=[("ppA", 0)], writes=[("ascr",)])
                S.op("act", lambda en: en.activation(out=gl_sb[:, :], in_=PA(1), func=AF.Gelu),
                     reads=[("ppA", 1)], writes=[("gel",)])
                S.op("dve", lambda en, c0=c0: en.tensor_tensor(
                         h2_sb[:, c0:c0 + 512], gl_sb[:, :], ascr_sb[:, :], ALU.mult),
                     reads=[("gel",), ("ascr",)], writes=[("h2", ci)])
            if K3B_STAGE < 4:
                continue
            h2reads = [("h2", ci) for ci in range(6)]
            for c in range(24):
                slot = tp_ps[:, (c % 8) * 128:(c % 8) * 128 + 128]
                S.op("pe", lambda en, c=c, slot=slot: en.transpose(
                         slot, h2_sb[:, c * 128:(c + 1) * 128], ident_sb[:, :]),
                     reads=h2reads + [("ident",)], writes=[("tp",)])
                S.op("act", lambda en, c=c, slot=slot: en.copy(
                         h2T_sb[:, c * 128:(c + 1) * 128], slot),
                     reads=[("tp",)], writes=[("h2T", c)])
            if K3B_STAGE < 5:
                continue
            for half in range(2):
                for c in range(24):
                    S.op("pe", lambda en, c=c, half=half: en.matmul(
                             PA(half), h2T_sb[:, c * 128:(c + 1) * 128],
                             w2_sb[c][:, half * 512:(half + 1) * 512],
                             start=(c == 0), stop=False),
                         reads=[("h2T", c), ("w2", c)], writes=[("ppA", half)])
                S.op("pe", lambda en, half=half: en.matmul(
                         PA(half), onesrb_sb[:, :], b2_sb[:, half * 512:(half + 1) * 512],
                         start=False, stop=True),
                     reads=[("onerb",), ("b2",)], writes=[("ppA", half)])
                S.op("dve", lambda en, half=half: en.tensor_tensor(
                         oev[:, half * 512:(half + 1) * 512], PA(half),
                         xa_t[:, half * 512:(half + 1) * 512], ALU.add),
                     reads=[("ppA", half), ("xat",)], writes=[("oev", half)])
            if L == 0:
                S.dma("qp", out_f[b, tf * 128:(tf + 1) * 128, :], oev[:, :],
                      reads=[("oev", 0), ("oev", 1)])
                S.op("act", lambda en: en.copy(obf_sb[:, :], oev[:, :]),
                     reads=[("oev", 0), ("oev", 1)], writes=[("obf",)])
                S.dma("qp", out_b[b, tf * 128:(tf + 1) * 128, :], obf_sb[:, :],
                      reads=[("obf",)])
            else:
                S.op("act", lambda en: en.activation(out=g0_sb[:, :], in_=oev[:, :],
                                                     func=AF.Silu),
                     reads=[("oev", 0), ("oev", 1)], writes=[("g0",)])
                halves = [(gl_sb, ("gel",)), (ascr_sb, ("ascr",))]
                for i in range(8):
                    buf = s0l_sb[i % 2]
                    bkeys = s0l_keys[i % 2]
                    S.dma("qs", buf[:, :], s0p_d[i, b, tf * 128:(tf + 1) * 128, :],
                          writes=bkeys)
                    for hf, (acc, ak) in enumerate(halves):
                        if i == 0:
                            S.op("pool", lambda en, buf=buf, acc=acc, hf=hf: en.tensor_copy(
                                     acc[:, :], buf[:, hf * 512:(hf + 1) * 512]),
                                 reads=bkeys, writes=[ak])
                        else:
                            S.op("pool", lambda en, buf=buf, acc=acc, hf=hf: en.tensor_tensor(
                                     acc[:, :], acc[:, :], buf[:, hf * 512:(hf + 1) * 512],
                                     ALU.add),
                                 reads=bkeys, writes=[ak])
                S.dma("qa", x0_sb[:, :], x0_d[b, tf * 128:(tf + 1) * 128, :],
                      writes=[("xat",)])
                for hf, (acc, ak) in enumerate(halves):
                    S.op("dve", lambda en, acc=acc, hf=hf: en.tensor_tensor(
                             acc[:, :], g0_sb[:, hf * 512:(hf + 1) * 512], acc[:, :],
                             ALU.mult),
                         reads=[("g0",)], writes=[ak])
                    S.op("dve", lambda en, acc=acc, hf=hf: en.tensor_tensor(
                             hp_sb[:, hf * 512:(hf + 1) * 512],
                             x0_sb[:, hf * 512:(hf + 1) * 512], acc[:, :], ALU.add),
                         reads=[("xat",), ak], writes=[("hp", hf)])
                S.dma("qp", out_f[b, tf * 128:(tf + 1) * 128, :], hp_sb[:, :],
                      reads=[("hp", 0), ("hp", 1)])

    stats_ = S.stats()
    S.finalize()
    esB.close()
    es.close()
    return nc, stats_


def rot_tables():
    DH, WINDOW, N = 64, 512, 8192
    half = np.arange(0, DH, 2, dtype=np.float64)
    inv_freq = 1.0 / 10000.0 ** (half / DH)
    t = np.arange(N, dtype=np.float64)
    freqs = np.concatenate([t[:, None] * inv_freq[None, :]] * 2, axis=-1)
    base = (half + 0.4 * DH) / (1.4 * DH)
    power = (t - N // 2) / float(WINDOW // 2)
    scale = np.concatenate([base[None, :] ** power[:, None]] * 2, axis=-1)
    cos, sin = np.cos(freqs), np.sin(freqs)
    sgn = np.ones(DH); sgn[:32] = -1.0
    A_q = cos * scale
    B_q = sin * scale * sgn
    A_k = cos / scale
    B_k = sin / scale * sgn
    tab = np.stack([A_q, B_q, A_k, B_k]).astype(np.float32)
    return np.ascontiguousarray(np.tile(tab, (1, 1, 8)))


def k3_inputs(layer, xn_bf16, xin2_f32, inp, s0p_by_core=None, x0=None,
              xin2_bf16=None):
    L = layer
    rot = rot_tables()
    wqkv = np.ascontiguousarray(inp["L0_ffn_wqkv"][L].astype(np.float32))
    qs = np.ascontiguousarray(np.tile(inp["L0_ffn_q_scale"][L].astype(np.float32), 8)[None, :])
    ks = np.ascontiguousarray(np.tile(inp["L0_ffn_k_scale"][L].astype(np.float32), 8)[None, :])
    wo = np.ascontiguousarray(inp["L0_ffn_wo"][L].astype(np.float32))
    lg = np.ascontiguousarray(inp["L0_ffn_ln_g"][L].astype(np.float32).reshape(8, 128).T)
    lb = np.ascontiguousarray(inp["L0_ffn_ln_b"][L].astype(np.float32).reshape(8, 128).T)
    w1r = inp["L0_ffn_w1"][L].astype(np.float32)
    b1r = inp["L0_ffn_b1"][L].astype(np.float32)
    w1 = np.zeros((1024, 6144), np.float32)
    w1[:, 0:2730] = w1r[:, 0:2730]
    w1[:, 3072:3072 + 2730] = w1r[:, 2730:5460]
    b1 = np.zeros((1, 6144), np.float32)
    b1[0, 0:2730] = b1r[0:2730]
    b1[0, 3072:3072 + 2730] = b1r[2730:5460]
    w2 = np.zeros((3072, 1024), np.float32)
    w2[0:2730] = inp["L0_ffn_w2"][L].astype(np.float32)
    w2 = np.ascontiguousarray(w2)
    w1 = np.ascontiguousarray(w1)
    b1 = np.ascontiguousarray(b1)
    b2 = np.ascontiguousarray(inp["L0_ffn_b2"][L].astype(np.float32)[None, :])

    def halo_slice(arr, start, dtype):
        out = np.zeros((2, HIN, 1024), dtype=dtype)
        lo = start - 1152
        src_lo = max(0, lo)
        out[:, src_lo - lo:, :] = arr[:, src_lo:start + 1024, :]
        return out

    maps = []
    for core in range(8):
        start = core * 1024
        m = {"wqkv": wqkv, "qs_row": qs, "ks_row": ks, "wo": wo,
             "lg_col": lg, "lb_col": lb, "w1": w1, "b1_row": b1,
             "w2": w2, "b2_row": b2}
        rlo = start - 1024
        rt = np.zeros((4, 2048, 512), np.float32)
        src_lo = max(0, rlo)
        rt[:, src_lo - rlo:, :] = rot[:, src_lo:start + 1024, :]
        m["rot"] = np.ascontiguousarray(rt)
        pw = np.zeros((128, NWIN), np.float32)
        for wr in range(1, 4):
            wg = start // 512 - 2 + wr
            pw[:, wr - 1] = 1.0 if wg >= 1 else 0.0
        m["pw"] = pw
        if L == 0:
            m["xin_bf"] = np.ascontiguousarray(halo_slice(xn_bf16, start, bfd))
        else:
            m["xin_f"] = np.ascontiguousarray(halo_slice(xin2_f32, start, np.float32))
            m["xin_bf"] = np.ascontiguousarray(halo_slice(xin2_bf16, start, bfd))
            FT1 = 9
            sl = np.zeros((8, 2, FT1 * 128, 1024), np.float32)
            lo = start - 128
            src_lo = max(0, lo)
            for i in range(8):
                sl[i, :, src_lo - lo:, :] = s0p_by_core[i][:, src_lo:start + 1024, :]
            m["s0p_in"] = np.ascontiguousarray(sl)
            xs = np.zeros((2, FT1 * 128, 1024), np.float32)
            xs[:, src_lo - lo:, :] = x0[:, src_lo:start + 1024, :]
            m["x0_in"] = np.ascontiguousarray(xs)
        maps.append(m)
    return maps


NT4 = 12

def build_k4():
    nc = bass.Bass("TRN2", target_bir_lowering=False)
    es = ExitStack()
    S = Sched(nc, es)

    x1_d = nc.dram_tensor("x1_in", [2, NT4 * 128, 1024], F32, kind="ExternalInput").ap()
    lnw_d = nc.dram_tensor("lnw_col", [128, 8], F32, kind="ExternalInput").ap()
    lnb_d = nc.dram_tensor("lnb_col", [128, 8], F32, kind="ExternalInput").ap()
    wslow_d = nc.dram_tensor("wslow", [1024, 3584], F32, kind="ExternalInput").ap()
    wout_d = nc.dram_tensor("wout", [1024, 1024], F32, kind="ExternalInput").ap()
    wrT_d = nc.dram_tensor("wrT", [1024, 1024], F32, kind="ExternalInput").ap()
    wkT_d = nc.dram_tensor("wkT", [1024, 1024], F32, kind="ExternalInput").ap()
    wvT_d = nc.dram_tensor("wvT", [1024, 1024], F32, kind="ExternalInput").ap()
    mur_d = nc.dram_tensor("mur_col", [128, 8], F32, kind="ExternalInput").ap()
    muk_d = nc.dram_tensor("muk_col", [128, 8], F32, kind="ExternalInput").ap()
    out_d = nc.dram_tensor("out", [2, NT4 * 128, 1024], F32, kind="ExternalOutput").ap()

    s1_sp = nc.dram_tensor("s1_sp", [2, NT4 * 128, 1024], F32, kind="Internal").ap()
    xn1b_sp = nc.dram_tensor("xn1b_sp", [2, NT4 * 128 + 1, 1024], BF16, kind="Internal").ap()

    ident_h = nc.inline_tensor(np.eye(128, dtype=bfd), "identb")
    sbt = lambda n, sh, dt: es.enter_context(nc.sbuf_tensor(n, sh, dt))
    pst = lambda n, sh, dt=F32: es.enter_context(nc.psum_tensor(n, sh, dt))

    eps_sb = sbt("eps_sb", [128, 1], F32)
    tiny_sb = sbt("tiny_sb", [128, 1], F32)
    stat = sbt("stat", [128, 16], F32)
    nrm4 = sbt("nrm4", [128, 16], F32)
    dots = sbt("dots", [128, 64], F32)
    bt_sb = sbt("bt_sb", [128, 32], F32)
    ident_sb = sbt("ident_sb", [128, 128], BF16)
    lnw_sb = sbt("lnw_sb", [128, 8], F32)
    lnb_sb = sbt("lnb_sb", [128, 8], F32)
    mur_sb = sbt("mur_sb", [128, 8], F32)
    muk_sb = sbt("muk_sb", [128, 8], F32)
    S.op("dve", lambda e: e.memset(eps_sb[:, :], 1e-5), writes=[("eps",)])
    S.op("dve", lambda e: e.memset(tiny_sb[:, :], 1e-38), writes=[("tiny",)])
    S.dma("qp", ident_sb[:, :], ident_h.ap(), writes=[("ident",)])
    S.dma("qs", lnw_sb[:, :], lnw_d, writes=[("lnw",)])
    S.dma("qs", lnb_sb[:, :], lnb_d, writes=[("lnb",)])
    S.dma("qs", mur_sb[:, :], mur_d, writes=[("mur",)])
    S.dma("qs", muk_sb[:, :], muk_d, writes=[("muk",)])

    ppA = pst("ppA", [128, 1024])
    tp_ps = pst("tp_ps", [128, 1024], BF16)
    PA = lambda h: ppA[:, h * 512:(h + 1) * 512]

    es1 = ExitStack()
    sb1 = lambda n, sh, dt: es1.enter_context(nc.sbuf_tensor(n, sh, dt))
    ws_sb = [sb1(f"ws{e}", [128, 3584], BF16) for e in range(8)]
    wo_sb = [sb1(f"wog{e}", [128, 1024], BF16) for e in range(8)]
    x1t = sb1("x1t", [128, 1024], F32)
    xn1f = sb1("xn1f", [128, 1024], F32)
    xn1b = sb1("xn1b", [128, 1024], BF16)
    hbf = sb1("hbf", [128, 1024], BF16)
    hT = sb1("hT", [128, 1024], BF16)
    qkvb = [sb1(f"qkvb{b}", [128, 3584], F32) for b in range(2)]
    q0h = sb1("q0h", [128, 1024], F32)
    k0h = sb1("k0h", [128, 1024], F32)
    q1h = sb1("q1h", [128, 1024], F32)
    k1h = sb1("k1h", [128, 1024], F32)
    elu_t = sb1("elu_t", [128, 1024], F32)
    prod = sb1("prod", [128, 1024], F32)
    d0_sb = sb1("d0_sb", [128, 1024], F32)
    u_sb = sb1("u_sb", [128, 1024], F32)
    o_sb = sb1("o_sb", [128, 1024], F32)
    obf = sb1("obf", [128, 1024], BF16)
    oT = sb1("oT", [128, 1024], BF16)
    s1ev = sb1("s1ev", [128, 1024], F32)

    for e in range(8):
        S.dma("qp", ws_sb[e][:, :], wslow_d[e * 128:(e + 1) * 128, :], writes=[("ws", e)])
        S.dma("qp", wo_sb[e][:, :], wout_d[e * 128:(e + 1) * 128, :], writes=[("wog", e)])

    KEY = {id(q0h): ("q0h",), id(k0h): ("k0h",), id(q1h): ("q1h",), id(k1h): ("k1h",)}
    H16 = lambda buf: buf[:, :].rearrange("p (h s) -> p h s", h=16)

    xn1f_keep = [sb1(f"xn1k{b}", [128, 1024], F32) for b in range(2)]

    for tau in range(NT4):
        r0 = tau * 128
        for b in range(2):
            qb = qkvb[b]
            qbk = ("qkvb", b)
            S.dma("qs", x1t[:, :], x1_d[b, r0:r0 + 128, :], writes=[("x1t",)])
            _ln_stats(S, stat, eps_sb, x1t[:, :], [("x1t",)], prod[:, :], "l4",
                      scr_keys=[("prod",)])
            xkeep = xn1f_keep[b]
            S.op("dve", lambda e, xkeep=xkeep: e.tensor_scalar(
                     xkeep[:, :], x1t[:, :], stat[:, 6:7], stat[:, 7:8],
                     ALU.mult, ALU.add),
                 reads=[("x1t",), ("l4", 6), ("l4", 7)], writes=[("xn1k", b)])
            S.op("act", lambda e, xkeep=xkeep: e.copy(xn1b[:, :], xkeep[:, :]),
                 reads=[("xn1k", b)], writes=[("xn1b",)])
            S.dma("qa", xn1b_sp[b, 1 + r0:1 + r0 + 128, :], xn1b[:, :],
                  reads=[("xn1b",)], writes=[("xnsp", b, tau)])
            _ln_stats(S, stat, eps_sb, xkeep[:, :], [("xn1k", b)], prod[:, :], "l5",
                      scr_keys=[("prod",)])
            S.op("dve", lambda e, xkeep=xkeep: e.tensor_scalar(
                     hbf[:, :], xkeep[:, :], stat[:, 6:7], stat[:, 7:8],
                     ALU.mult, ALU.add),
                 reads=[("xn1k", b), ("l5", 6), ("l5", 7)], writes=[("hbf",)])
            for c in range(8):
                slot = tp_ps[:, c * 128:(c + 1) * 128]
                S.op("pe", lambda e, c=c, slot=slot: e.transpose(
                         slot, hbf[:, c * 128:(c + 1) * 128], ident_sb[:, :]),
                     reads=[("hbf",), ("ident",)], writes=[("tp",)])
                S.op("act", lambda e, c=c, slot=slot: e.copy(hT[:, c * 128:(c + 1) * 128], slot),
                     reads=[("tp",)], writes=[("hTc", c)])
                S.op("dve", lambda e, c=c: e.tensor_scalar(
                         hT[:, c * 128:(c + 1) * 128], hT[:, c * 128:(c + 1) * 128],
                         lnw_sb[:, c:c + 1], lnb_sb[:, c:c + 1], ALU.mult, ALU.add),
                     reads=[("lnw",), ("lnb",)], writes=[("hTc", c)])
            hreads = [("hTc", c) for c in range(8)]
            for cc in range(7):
                pp = PA(cc % 2)
                for e in range(8):
                    S.op("pe", lambda en, e=e, cc=cc, pp=pp: en.matmul(
                             pp, hT[:, e * 128:(e + 1) * 128],
                             ws_sb[e][:, cc * 512:(cc + 1) * 512],
                             start=(e == 0), stop=(e == 7)),
                         reads=hreads + [("ws", e)], writes=[("ppA", cc % 2)])
                S.op("act", lambda en, cc=cc, qb=qb, pp=pp: en.copy(
                         qb[:, cc * 512:(cc + 1) * 512], pp),
                     reads=[("ppA", cc % 2)], writes=[qbk])
        v0 = qkvb[0][:, :].rearrange("p (h s) -> p h s", h=16)
        v1 = qkvb[1][:, :].rearrange("p (h s) -> p h s", h=16)
        S.op("act", lambda e: e.activation(out=bt_sb[:, 0:16].unsqueeze(2),
                                           in_=v0[:, :, 192:193], func=AF.Sigmoid),
             reads=[("qkvb", 0)], writes=[("bt",)])
        S.op("act", lambda e: e.activation(out=bt_sb[:, 16:32].unsqueeze(2),
                                           in_=v1[:, :, 192:193], func=AF.Sigmoid),
             reads=[("qkvb", 1)], writes=[("bt",)])
        for (bb, qk, dst) in ((0, 0, q0h), (0, 1, k0h), (1, 0, q1h), (1, 1, k1h)):
            sv = (v0 if bb == 0 else v1)[:, :, qk * 64:(qk + 1) * 64]
            qbk = ("qkvb", bb)
            dk = KEY[id(dst)]
            S.op("dve", lambda e, sv=sv, dst=dst: e.tensor_scalar(
                     H16(dst), sv, 0.0, None, ALU.min),
                 reads=[qbk], writes=[dk])
            S.op("act", lambda e, dst=dst: e.activation(out=dst[:, :], in_=dst[:, :],
                                                        func=AF.Exp),
                 reads=[dk], writes=[dk])
            S.op("dve", lambda e, sv=sv: e.tensor_scalar(
                     H16(elu_t), sv, 0.0, None, ALU.max),
                 reads=[qbk], writes=[("elu_t",)])
            S.op("dve", lambda e, dst=dst: e.tensor_tensor(
                     dst[:, :], dst[:, :], elu_t[:, :], ALU.add),
                 reads=[("elu_t",), dk], writes=[dk])
            S.op("dve", lambda e, dst=dst: e.reduce_sum(nrm4[:, :], H16(dst), axis=AX.X),
                 reads=[dk], writes=[("nr4",)], hazard=True)
            S.op("act", lambda e: e.activation(out=nrm4[:, :], in_=nrm4[:, :],
                                               func=AF.Ln, bias=tiny_sb[:, 0:1]),
                 reads=[("tiny",), ("nr4",)], writes=[("nr4",)])
            S.op("act", lambda e: e.activation(out=nrm4[:, :], in_=nrm4[:, :],
                                               func=AF.Exp, scale=-1.0),
                 reads=[("nr4",)], writes=[("nr4",)])
            for h in range(16):
                S.op("dve", lambda e, dst=dst, h=h: e.tensor_scalar(
                         dst[:, h * 64:(h + 1) * 64], dst[:, h * 64:(h + 1) * 64],
                         nrm4[:, h:h + 1], None, ALU.mult),
                     reads=[("nr4",)], writes=[dk])
        for (i2, ua, ub) in ((0, k0h, q0h), (1, k0h, q1h), (2, k0h, k1h), (3, k1h, q1h)):
            S.op("dve", lambda e, ua=ua, ub=ub: e.tensor_tensor(
                     prod[:, :], ua[:, :], ub[:, :], ALU.mult),
                 reads=[KEY[id(ua)], KEY[id(ub)]], writes=[("prod",)])
            S.op("dve", lambda e, i2=i2: e.reduce_sum(
                     dots[:, i2 * 16:(i2 + 1) * 16], H16(prod), axis=AX.X),
                 reads=[("prod",)], writes=[("dots",)], hazard=True)
        S.op("act", lambda e: e.copy(dots[:, :], dots[:, :]),
             reads=[("dots",)], writes=[("dots2",)])
        for h in range(16):
            S.op("dve", lambda e, h=h, v0=v0: e.tensor_scalar(
                     d0_sb[:, h * 64:(h + 1) * 64], v0[:, h, 128:192],
                     bt_sb[:, h:h + 1], None, ALU.mult),
                 reads=[("qkvb", 0), ("bt",)], writes=[("d0",)])
        for h in range(16):
            S.op("dve", lambda e, h=h: e.tensor_scalar(
                     o_sb[:, h * 64:(h + 1) * 64], d0_sb[:, h * 64:(h + 1) * 64],
                     dots[:, h:h + 1], None, ALU.mult),
                 reads=[("d0",), ("dots2",)], writes=[("o0",)])
        S.op("act", lambda e: e.copy(obf[:, :], o_sb[:, :]),
             reads=[("o0",)], writes=[("obf",)])
        def s1_emit(bq):
            for c in range(8):
                slot = tp_ps[:, c * 128:(c + 1) * 128]
                S.op("pe", lambda e, c=c, slot=slot: e.transpose(
                         slot, obf[:, c * 128:(c + 1) * 128], ident_sb[:, :]),
                     reads=[("obf",), ("ident",)], writes=[("tp",)])
                S.op("act", lambda e, c=c, slot=slot: e.copy(oT[:, c * 128:(c + 1) * 128], slot),
                     reads=[("tp",)], writes=[("oTc", c)])
            oreads = [("oTc", c) for c in range(8)]
            for half in range(2):
                for c in range(8):
                    S.op("pe", lambda e, c=c, half=half: e.matmul(
                             PA(half), oT[:, c * 128:(c + 1) * 128],
                             wo_sb[c][:, half * 512:(half + 1) * 512],
                             start=(c == 0), stop=(c == 7)),
                         reads=oreads + [("wog", c)], writes=[("ppA", half)])
                S.op("dve", lambda e, half=half, bq=bq: e.tensor_tensor(
                         s1ev[:, half * 512:(half + 1) * 512], PA(half),
                         xn1f_keep[bq][:, half * 512:(half + 1) * 512], ALU.add),
                     reads=[("ppA", half), ("xn1k", bq)], writes=[("s1ev", half)])
            S.dma("qp", s1_sp[bq, r0:r0 + 128, :], s1ev[:, :],
                  reads=[("s1ev", 0), ("s1ev", 1)], writes=[("s1sp", bq, tau)])
        s1_emit(0)
        for h in range(16):
            S.op("dve", lambda e, h=h: e.tensor_scalar(
                     u_sb[:, h * 64:(h + 1) * 64], d0_sb[:, h * 64:(h + 1) * 64],
                     dots[:, 32 + h:33 + h], None, ALU.mult),
                 reads=[("d0",), ("dots2",)], writes=[("u",)])
        S.op("dve", lambda e: e.tensor_tensor(
                 H16(prod), v1[:, :, 128:192], H16(u_sb), ALU.subtract),
             reads=[("qkvb", 1), ("u",)], writes=[("prod",)])
        for h in range(16):
            S.op("dve", lambda e, h=h: e.tensor_scalar(
                     u_sb[:, h * 64:(h + 1) * 64], prod[:, h * 64:(h + 1) * 64],
                     bt_sb[:, 16 + h:17 + h], None, ALU.mult),
                 reads=[("prod",), ("bt",)], writes=[("u",)])
        for h in range(16):
            S.op("dve", lambda e, h=h: e.tensor_scalar(
                     o_sb[:, h * 64:(h + 1) * 64], d0_sb[:, h * 64:(h + 1) * 64],
                     dots[:, 16 + h:17 + h], None, ALU.mult),
                 reads=[("d0",), ("dots2",)], writes=[("o0",)])
        for h in range(16):
            S.op("dve", lambda e, h=h: e.tensor_scalar(
                     elu_t[:, h * 64:(h + 1) * 64], u_sb[:, h * 64:(h + 1) * 64],
                     dots[:, 48 + h:49 + h], None, ALU.mult),
                 reads=[("u",), ("dots2",)], writes=[("elu_t",)])
        S.op("dve", lambda e: e.tensor_tensor(o_sb[:, :], o_sb[:, :], elu_t[:, :], ALU.add),
             reads=[("elu_t",), ("o0",)], writes=[("o0",)])
        S.op("act", lambda e: e.copy(obf[:, :], o_sb[:, :]),
             reads=[("o0",)], writes=[("obf",)])
        s1_emit(1)

    S.barrier_all()
    es1.close()
    es2 = ExitStack()
    sb2 = lambda n, sh, dt: es2.enter_context(nc.sbuf_tensor(n, sh, dt))
    wr_sb = [sb2(f"wr{e}", [128, 1024], BF16) for e in range(8)]
    wk_sb = [sb2(f"wk{e}", [128, 1024], BF16) for e in range(8)]
    wv_sb = [sb2(f"wv{e}", [128, 1024], BF16) for e in range(8)]
    xT_sb = [sb2(f"xT{e}", [128, 512], BF16) for e in range(8)]
    xpT_sb = [sb2(f"xpT{e}", [128, 512], BF16) for e in range(8)]
    xrT_sb = [sb2(f"xrT{e}", [128, 512], BF16) for e in range(8)]
    xkT_sb = [sb2(f"xkT{e}", [128, 512], BF16) for e in range(8)]
    kT2_sb = [sb2(f"kT2{e}", [128, 512], BF16) for e in range(8)]
    r_sb = sb2("r_sb", [128, 1024], F32)
    o2_sb = sb2("o2_sb", [128, 1024], F32)
    g1_sb = sb2("g1_sb", [128, 1024], F32)
    x1t2 = sb2("x1t2", [128, 1024], F32)
    s1t = sb2("s1t", [128, 1024], F32)
    fin = sb2("fin", [128, 1024], F32)
    scr2a = sb2("scr2a", [128, 512], F32)

    for e in range(8):
        S.dma("qp", wr_sb[e][:, :], wrT_d[e * 128:(e + 1) * 128, :], writes=[("wr", e)])
        S.dma("qp", wk_sb[e][:, :], wkT_d[e * 128:(e + 1) * 128, :], writes=[("wk", e)])
        S.dma("qp", wv_sb[e][:, :], wvT_d[e * 128:(e + 1) * 128, :], writes=[("wv", e)])

    for b in range(2):
        for pan in range(3):
            p0 = pan * 512
            xnreads = [("xnsp", b, t) for t in range(pan * 4, pan * 4 + 4)]
            xnreads_m1 = xnreads + ([("xnsp", b, pan * 4 - 1)] if pan > 0 else [])
            for e in range(8):
                S.dma_transpose("qa", xT_sb[e][:, :],
                                xn1b_sp[b, 1 + p0:1 + p0 + 512, e * 128:(e + 1) * 128],
                                reads=xnreads, writes=[("xTe", e)])
                S.dma_transpose("qa", xpT_sb[e][:, :],
                                xn1b_sp[b, p0:p0 + 512, e * 128:(e + 1) * 128],
                                reads=xnreads_m1, writes=[("xpTe", e)])
                S.op("dve", lambda en, e=e: en.tensor_tensor(
                         xrT_sb[e][:, :], xT_sb[e][:, :], xpT_sb[e][:, :], ALU.subtract),
                     reads=[("xTe", e), ("xpTe", e)], writes=[("xrTe", e)])
                S.op("dve", lambda en, e=e: en.scalar_tensor_tensor(
                         xkT_sb[e][:, :], xrT_sb[e][:, :], muk_sb[:, e:e + 1],
                         xpT_sb[e][:, :], ALU.mult, ALU.add),
                     reads=[("xrTe", e), ("muk",), ("xpTe", e)], writes=[("xkTe", e)])
                S.op("dve", lambda en, e=e: en.scalar_tensor_tensor(
                         xrT_sb[e][:, :], xrT_sb[e][:, :], mur_sb[:, e:e + 1],
                         xpT_sb[e][:, :], ALU.mult, ALU.add),
                     reads=[("mur",), ("xpTe", e), ("xrTe", e)], writes=[("xrTe", e)])
            for dchunk in range(8):
                for e in range(8):
                    S.op("pe", lambda en, e=e, dchunk=dchunk: en.matmul(
                             PA(0), wk_sb[e][:, dchunk * 128:(dchunk + 1) * 128],
                             xkT_sb[e][:, :], start=(e == 0), stop=(e == 7)),
                         reads=[("wk", e), ("xkTe", e)], writes=[("ppA", 0)])
                S.op("dve", lambda en: en.tensor_scalar(scr2a[:, :], PA(0),
                                                        0.0, None, ALU.max),
                     reads=[("ppA", 0)], writes=[("scr2a",)])
                S.op("act", lambda en, dchunk=dchunk: en.activation(
                         out=kT2_sb[dchunk][:, :], in_=scr2a[:, :], func=AF.Square),
                     reads=[("scr2a",)], writes=[("kT2e", dchunk)])
            for tl in range(4):
                t0 = tl * 128
                for half in range(2):
                    for e in range(8):
                        S.op("pe", lambda en, e=e, half=half, t0=t0: en.matmul(
                                 PA(half), xrT_sb[e][:, t0:t0 + 128],
                                 wr_sb[e][:, half * 512:(half + 1) * 512],
                                 start=(e == 0), stop=(e == 7)),
                             reads=[("xrTe", e), ("wr", e)], writes=[("ppA", half)])
                    S.op("act", lambda en, half=half: en.activation(
                             out=r_sb[:, half * 512:(half + 1) * 512], in_=PA(half),
                             func=AF.Sigmoid),
                         reads=[("ppA", half)], writes=[("r", half)])
                for half in range(2):
                    for e in range(8):
                        S.op("pe", lambda en, e=e, half=half, t0=t0: en.matmul(
                                 PA(half), kT2_sb[e][:, t0:t0 + 128],
                                 wv_sb[e][:, half * 512:(half + 1) * 512],
                                 start=(e == 0), stop=(e == 7)),
                             reads=[("kT2e", e), ("wv", e)], writes=[("ppA", half)])
                    S.op("act", lambda en, half=half: en.copy(
                             o2_sb[:, half * 512:(half + 1) * 512], PA(half)),
                         reads=[("ppA", half)], writes=[("o2", half)])
                S.op("dve", lambda en: en.tensor_tensor(g1_sb[:, :], r_sb[:, :],
                                                        o2_sb[:, :], ALU.mult),
                     reads=[("r", 0), ("r", 1), ("o2", 0), ("o2", 1)], writes=[("g1",)])
                S.op("act", lambda en: en.activation(out=g1_sb[:, :], in_=g1_sb[:, :],
                                                     func=AF.Silu),
                     reads=[("g1",)], writes=[("g1",)])
                row = p0 + t0
                S.dma("qs", x1t2[:, :], x1_d[b, row:row + 128, :], writes=[("x1t2",)])
                S.dma("qs", s1t[:, :], s1_sp[b, row:row + 128, :],
                      reads=[("s1sp", b, row // 128)], writes=[("s1t",)])
                S.op("dve", lambda en: en.tensor_tensor(fin[:, :], s1t[:, :],
                                                        g1_sb[:, :], ALU.mult),
                     reads=[("s1t",), ("g1",)], writes=[("fin",)])
                S.op("dve", lambda en: en.tensor_tensor(fin[:, :], fin[:, :],
                                                        x1t2[:, :], ALU.add),
                     reads=[("x1t2",)], writes=[("fin",)])
                S.dma("qp", out_d[b, row:row + 128, :], fin[:, :], reads=[("fin",)])

    stats_ = S.stats()
    S.finalize()
    es2.close()
    es.close()
    return nc, stats_


def k4_inputs(x1_full, inp):
    wslow = np.zeros((1024, 3584), np.float32)
    wslow[:, 0:3088] = inp["L1_attn_w_slow"].astype(np.float32)
    m0 = {
        "wslow": np.ascontiguousarray(wslow),
        "wout": np.ascontiguousarray(inp["L1_attn_w_out"].astype(np.float32)),
        "wrT": np.ascontiguousarray(inp["L1_ffn_W_r"].astype(np.float32).T),
        "wkT": np.ascontiguousarray(inp["L1_ffn_W_k"].astype(np.float32).T),
        "wvT": np.ascontiguousarray(inp["L1_ffn_W_v"].astype(np.float32).T),
        "lnw_col": np.ascontiguousarray(inp["L1_attn_ln_w"].astype(np.float32).reshape(8, 128).T),
        "lnb_col": np.ascontiguousarray(inp["L1_attn_ln_b"].astype(np.float32).reshape(8, 128).T),
        "mur_col": np.ascontiguousarray(inp["L1_ffn_mu_r"].astype(np.float32).reshape(8, 128).T),
        "muk_col": np.ascontiguousarray(inp["L1_ffn_mu_k"].astype(np.float32).reshape(8, 128).T),
    }
    maps = []
    for core in range(8):
        start = core * 1024
        xs = np.zeros((2, NT4 * 128, 1024), np.float32)
        lo = start - 512
        src_lo = max(0, lo)
        xs[:, src_lo - lo:, :] = x1_full[:, src_lo:start + 1024, :]
        m = dict(m0)
        m["x1_in"] = np.ascontiguousarray(xs)
        maps.append(m)
    return maps


_PROGS = {}

def _progs():
    if not _PROGS:
        _PROGS["k12"] = build_k12()[0]
        _PROGS["k3a"] = build_k3(0)[0]
        _PROGS["k3b"] = build_k3(1)[0]
        _PROGS["k4"] = build_k4()[0]
    return _PROGS


def kernel(**inputs):
    from concourse.bass_utils import run_bass_kernel_spmd
    inp = {k: np.asarray(v, dtype=np.float32) for k, v in inputs.items()}
    x = inp["L0_attn_x"]
    progs = _progs()
    cores = list(range(8))

    maps = k12_inputs(x, inp["L0_attn_Wqkv"], inp["L0_attn_Wo"])
    r12 = run_bass_kernel_spmd(progs["k12"], maps, core_ids=cores)
    xn_bf = r12.results[0]["xn_out"].reshape(2, 8192, 1024)
    s0p = [r12.results[c]["s0p"].reshape(2, 8192, 1024) for c in range(8)]

    maps = k3_inputs(0, xn_bf, None, inp)
    r3a = run_bass_kernel_spmd(progs["k3a"], maps, core_ids=cores)
    xin2_f = np.concatenate([r3a.results[c]["out_f"] for c in range(8)], axis=1)
    xin2_b = np.concatenate([r3a.results[c]["out_b"] for c in range(8)], axis=1)

    maps = k3_inputs(1, None, xin2_f, inp, s0p_by_core=s0p, x0=x, xin2_bf16=xin2_b)
    r3b = run_bass_kernel_spmd(progs["k3b"], maps, core_ids=cores)
    x1 = np.concatenate([r3b.results[c]["out_f"][:, 128:, :] for c in range(8)], axis=1)

    maps = k4_inputs(x1, inp)
    r4 = run_bass_kernel_spmd(progs["k4"], maps, core_ids=cores)
    out = np.concatenate([r4.results[c]["out"][:, 512:, :] for c in range(8)], axis=1)
    return np.ascontiguousarray(out, dtype=np.float32)
